# version 17
# baseline (speedup 1.0000x reference)
import math
import numpy as np
import ml_dtypes

import concourse.bass as bass
import concourse.mybir as mybir
from concourse.bass_utils import run_bass_kernel_spmd

F32 = mybir.dt.float32
BF16 = mybir.dt.bfloat16
AF = mybir.ActivationFunctionType
ALU = mybir.AluOpType
AX = mybir.AxisListType

BC = 2
T = 2048
C = 2048
NH = 16
L = 128
INTER = 8192
RINTER = 1024
NE = 8
PANEL = 512
NTS = 4
LPN = T // PANEL
NPANEL = BC * LPN
NFC = C // 128
NIC = INTER // 128
NRIC = RINTER // 128
SCALE = 1.0 / math.sqrt(L)
EPS = 1e-5


DMA_SEMS = ({"xw", "ow", "sq", "sg", "wst"}
            | {f"L{s}{p}" for s in ("wd", "wp", "gw", "uw", "sd", "rg", "ru", "rd")
               for p in (0, 1, 2)}
            | {f"G{s}{p}" for s in ("wd", "wp", "gw", "uw", "sd", "rg", "ru", "rd")
               for p in (0, 1, 2)})


class Sched:
    def __init__(self):
        self.items = []
        self.cnt = {}
        self.wmax = {}
        self.sems = {}

    def add_sem(self, name, handle):
        self.sems[name] = handle
        self.cnt[name] = 0

    def emit(self, eng, fn, waits=(), inc=None, inc_by=1):
        need = []
        for tok in waits:
            if tok is None:
                continue
            s, v = tok
            if s in DMA_SEMS:
                v = self.cnt[s]
            key = (eng, s)
            if v > self.wmax.get(key, 0):
                need.append((s, v))
                self.wmax[key] = v
        tok = None
        if inc is not None:
            self.cnt[inc] += inc_by
            tok = (inc, self.cnt[inc])
        self.items.append((eng, fn, tuple(need), (inc, inc_by) if inc else None))
        return tok

    def replay(self, eng_name, eng_obj):
        for eng, fn, waits, inc in self.items:
            if eng != eng_name:
                continue
            for s, v in waits:
                eng_obj.wait_ge(self.sems[s], v)
            inst = fn(eng_obj)
            if inc is not None:
                inst.then_inc(self.sems[inc[0]], inc[1])


def maxtok(*toks):
    best = {}
    for t in toks:
        if t is None:
            continue
        if isinstance(t, list):
            for tt in t:
                if tt is None:
                    continue
                s, v = tt
                if v > best.get(s, 0):
                    best[s] = v
            continue
        s, v = t
        if v > best.get(s, 0):
            best[s] = v
    return [(s, v) for s, v in best.items()]


def build_program():
    nc = bass.Bass("TRN2", target_bir_lowering=False)

    x_in = nc.dram_tensor("x_in", [BC, T, C], F32, kind="ExternalInput").ap()
    w_lat = nc.dram_tensor("w_lat", [C, L], F32, kind="ExternalInput").ap()
    b_lat = nc.dram_tensor("b_lat", [1, L], F32, kind="ExternalInput").ap()
    w_d = nc.dram_tensor("w_d", [C, C], F32, kind="ExternalInput").ap()
    b_d = nc.dram_tensor("b_d", [1, C], F32, kind="ExternalInput").ap()
    w_proj = nc.dram_tensor("w_proj", [C, C], F32, kind="ExternalInput").ap()
    b_proj = nc.dram_tensor("b_proj", [1, C], F32, kind="ExternalInput").ap()
    rw_t = nc.dram_tensor("rw_t", [C, NE], F32, kind="ExternalInput").ap()
    gw_t = nc.dram_tensor("gw_t", [C, INTER], F32, kind="ExternalInput").ap()
    uw_t = nc.dram_tensor("uw_t", [C, INTER], F32, kind="ExternalInput").ap()
    sd_t = nc.dram_tensor("sd_t", [INTER, C], F32, kind="ExternalInput").ap()
    rg_t = nc.dram_tensor("rg_t", [C, RINTER], F32, kind="ExternalInput").ap()
    ru_t = nc.dram_tensor("ru_t", [C, RINTER], F32, kind="ExternalInput").ap()
    rd_t = nc.dram_tensor("rd_t", [RINTER, C], F32, kind="ExternalInput").ap()
    ident_in = nc.dram_tensor("ident_in", [128, 128], BF16, kind="ExternalInput").ap()
    tri_in = nc.dram_tensor("tri_in", [128, 128], BF16, kind="ExternalInput").ap()
    onesc_in = nc.dram_tensor("onesc_in", [128, 1], BF16, kind="ExternalInput").ap()
    onesr_in = nc.dram_tensor("onesr_in", [1, 128], F32, kind="ExternalInput").ap()

    out = nc.dram_tensor("out", [BC, T, C], F32, kind="ExternalOutput").ap()

    def scrt(name, nwin, els):
        return nc.dram_tensor(name, [nwin, 128, els], BF16, kind="Internal").ap()

    wd_scr = scrt("wd_scr", 8, 16 * 256)
    wp_scr = scrt("wp_scr", 8, 16 * 256)
    gu_scr = scrt("gu_scr", 64, 2 * 16 * 128)
    sd_scr = scrt("sd_scr", 32, 8 * 512)
    rgu_scr = scrt("rgu_scr", 8, 2 * 16 * 128)
    rd_scr = scrt("rd_scr", 4, 8 * 512)

    from contextlib import ExitStack
    es = ExitStack()

    def sb(name, shape, dt):
        return es.enter_context(nc.sbuf_tensor(name, shape, dt))

    wlat_win = sb("wlat_win", [128, NFC * L], BF16)
    rw_win = sb("rw_win", [128, NFC * NE], BF16)
    ident_s = sb("ident_s", [128, 128], BF16)
    tri_s = sb("tri_s", [128, 128], BF16)
    onesc_s = sb("onesc_s", [128, 1], BF16)
    onesr_s = sb("onesr_s", [1, 128], F32)
    blat_s = sb("blat_s", [128, 1], F32)
    bd_s = sb("bd_s", [128, NFC], F32)
    bproj_bc = sb("bproj_bc", [128, C], F32)
    klat_s = sb("klat_s", [128, T], BF16)
    ktok_s = sb("ktok_s", [128, T], BF16)
    stats = sb("stats", [128, 32], F32)
    stats2 = sb("stats2", [128, 16], F32)
    logits_s = sb("logits_s", [128, NTS * NE], F32)
    comb_s = sb("comb_s", [128, NTS], F32)
    sx = sb("sx", [128, NTS * C], F32)
    aof = sb("aof", [128, NTS * C], F32)
    xnfm = sb("xnfm", [128, NFC * PANEL], BF16)
    ovl = sb("ovl", [128, 32768], BF16)
    hr_s = sb("hr_s", [128, NRIC * PANEL], BF16)
    scrf = sb("scrf", [128, 1536], F32)
    warena = sb("warena", [128, 8192], BF16)

    Q = ovl[:, 0:8192]
    xntok = [ovl[:, 0:2048], ovl[:, 2048:4096]]
    Y = ovl[:, 8192:16384]
    PB = [ovl[:, 16384 + i * 512: 16384 + (i + 1) * 512] for i in range(2)]
    H = ovl[:, 0:32768]
    SG = [scrf[:, 0:512], scrf[:, 512:1024]]
    RBB = [scrf[:, 0:512], scrf[:, 512:1024]]
    rrow_s = scrf[0:1, 1024:1536]
    sqdump = ovl[:, 21504:23552]
    sqdump2 = ovl[:, 23552:25600]

    def psal(name, shape, dt):
        return es.enter_context(nc.psum_tensor(name, shape, dt))

    class PBank:
        def __init__(self, ap):
            self.ap = ap
            self.war = None

    P = [PBank(psal(f"p{i}", [128, 512], F32)[:, :]) for i in range(6)]
    PT = [PBank(psal("pt0", [128, 1024], BF16)[:, 0:128]),
          PBank(psal("pt1", [128, 1024], BF16)[:, 0:128])]

    S = Sched()
    for s in sorted(DMA_SEMS) + ["pe", "act", "dve"]:
        S.add_sem(s, es.enter_context(nc.semaphore(s)))

    def dma(eng, sem, out_ap, in_ap, waits=()):
        return S.emit(eng, lambda e, o=out_ap, i=in_ap: e.dma_start(out=o, in_=i),
                      waits=waits, inc=sem, inc_by=16)

    def mm(bank, out_ap, lhsT, rhs, start, stop, waits=(), inc=None):
        w = list(waits)
        if start and bank is not None:
            w.append(bank.war)
        return S.emit("tensor",
                      lambda e, o=out_ap, l=lhsT, r=rhs, st=start, sp=stop:
                      e.matmul(o, l, r, start=st, stop=sp),
                      waits=maxtok(*w), inc=inc)

    def tpose(bank, in_ap, waits=()):
        w = list(waits) + [bank.war]
        return S.emit("tensor",
                      lambda e, o=bank.ap, i=in_ap: e.transpose(o, i, ident_s[:, :]),
                      waits=maxtok(*w), inc="pe")

    def act(out_ap, in_ap, func, bias=None, scale=None, accum=None, waits=(),
            reads=()):
        def f(e, o=out_ap, i=in_ap, fu=func, b=bias, sc=scale, a=accum):
            kw = {}
            if b is not None:
                kw["bias"] = b
            if sc is not None:
                kw["scale"] = sc
            if a is not None:
                kw["accum_out"] = a
            return e.activation(o, i, fu, **kw)
        t = S.emit("scalar", f, waits=maxtok(*waits), inc="act")
        for b in reads:
            b.war = t
        return t

    def dve(fn, waits=(), reads=()):
        t = S.emit("vector", fn, waits=maxtok(*waits), inc="dve")
        for b in reads:
            b.war = t
        return t

    qwar = [[], [], [], [], [], []]
    qap = [warena[:, i * 2048:(i + 1) * 2048] for i in range(4)]
    dslot = {0: warena[:, 0:4096], 1: warena[:, 4096:8192],
             2: xnfm[:, 0:4096]}

    class WStream:
        def __init__(self, name, scratch, nwin, win_els, quarters, src_fn,
                     sems, src_fn2=None):
            self.name = name
            self.scratch = scratch
            self.nwin = nwin
            self.win_els = win_els
            self.quarters = quarters
            self.src_fn = src_fn
            self.src_fn2 = src_fn2
            self.sems = sems
            self.load_tok = [None] * len(quarters)
            self.store_tok = [None] * nwin
            self.fetched = -1

        def nbuf(self):
            return len(self.quarters)

        def buf_ap(self, w):
            qs = self.quarters[w % self.nbuf()]
            if len(qs) == 1:
                return qap[qs[0]][:, 0:self.win_els]
            return dslot[{0: 0, 2: 1, 4: 2}[qs[0]]][:, 0:self.win_els]

        def ensure(self, w, panel):
            if w >= self.nwin or w <= self.fetched:
                return
            self.fetched = w
            par = w % self.nbuf()
            qs = self.quarters[par]
            buf = self.buf_ap(w)
            wars = [t for q in qs for t in qwar[q]]
            if panel == 0:
                gsem = "G" + self.sems[par][1:]
                if self.src_fn2 is not None:
                    half = self.win_els // 2
                    dma("gpsimd", gsem, buf[:, 0:half], self.src_fn(w),
                        waits=maxtok(*wars))
                    t = dma("gpsimd", gsem, buf[:, half:2 * half],
                            self.src_fn2(w), waits=maxtok(*wars))
                else:
                    t = dma("gpsimd", gsem, buf, self.src_fn(w),
                            waits=maxtok(*wars))
                self.load_tok[par] = t
                st = dma("sync", "wst", self.scratch[w], buf, waits=[t])
                self.store_tok[w] = st
                for q in qs:
                    qwar[q] = [st]
            else:
                t = dma("sync", self.sems[par], buf, self.scratch[w],
                        waits=maxtok(*(wars + [self.store_tok[w]])))
                self.load_tok[par] = t
                for q in qs:
                    qwar[q] = []

        def win(self, w):
            return self.buf_ap(w), self.load_tok[w % self.nbuf()]

        def consume(self, w, tok):
            for q in self.quarters[w % self.nbuf()]:
                qwar[q].append(tok)

        def reset(self):
            self.fetched = -1

    def colwin(src, kc0, nkc, n0, wn):
        return src[kc0 * 128:(kc0 + nkc) * 128, n0:n0 + wn].rearrange(
            "(k p) n -> p k n", p=128)

    wd_st = WStream("wd", wd_scr, 8, 16 * 256, [(0, 1), (2, 3)],
                    lambda w: colwin(w_d, 0, 16, w * 256, 256),
                    sems=("Lwd0", "Lwd1"))
    wp_st = WStream("wp", wp_scr, 8, 16 * 256, [(0, 1), (2, 3), (4, 5)],
                    lambda w: colwin(w_proj, 0, 16, w * 256, 256),
                    sems=("Lwp0", "Lwp1", "Lwp2"))
    gu_st = WStream("gu", gu_scr, 64, 2 * 16 * 128, [(0, 1), (2, 3)],
                    lambda w: colwin(gw_t, 0, 16, w * 128, 128),
                    sems=("Lgw0", "Lgw1"),
                    src_fn2=lambda w: colwin(uw_t, 0, 16, w * 128, 128))
    sd_st = WStream("sd", sd_scr, 32, 8 * 512, [(0, 1), (2, 3), (4, 5)],
                    lambda w: colwin(sd_t, (w % 8) * 8, 8, (w // 8) * 512, 512),
                    sems=("Lsd0", "Lsd1", "Lsd2"))
    rgu_st = WStream("rgu", rgu_scr, 8, 2 * 16 * 128, [(0, 1), (2, 3)],
                     lambda w: colwin(rg_t, 0, 16, w * 128, 128),
                     sems=("Lrg0", "Lrg1"),
                     src_fn2=lambda w: colwin(ru_t, 0, 16, w * 128, 128))
    rd_st = WStream("rd", rd_scr, 4, 8 * 512, [(0, 1), (2, 3)],
                    lambda w: colwin(rd_t, 0, 8, w * 512, 512),
                    sems=("Lrd0", "Lrd1"))
    streams = [wd_st, wp_st, gu_st, sd_st, rgu_st, rd_st]

    t_id = dma("sync", "sq", ident_s[:, :], ident_in)
    t_tri = dma("sync", "sq", tri_s[:, :], tri_in)
    t_oc = dma("sync", "sq", onesc_s[:, :], onesc_in)
    t_or = dma("sync", "sq", onesr_s[:, :], onesr_in)
    t_wlat = dma("gpsimd", "sg", wlat_win[:, :].rearrange("p (k n) -> p k n", n=L),
                 colwin(w_lat, 0, 16, 0, L))
    t_rw = dma("gpsimd", "sg", rw_win[:, :].rearrange("p (k n) -> p k n", n=NE),
               colwin(rw_t, 0, 16, 0, NE))

    brow = aof[0:1, 0:2048]
    ones11 = onesr_s[0:1, 0:1]
    t_bd = dma("sync", "sq", brow, b_d, waits=[t_or])
    last = None
    for qc in range(NFC):
        last = mm(P[0] if qc == 0 else None, P[0].ap[:, qc:qc + 1],
                  brow[:, qc * 128:(qc + 1) * 128], ones11, True, True,
                  waits=[t_bd], inc="pe" if qc == NFC - 1 else None)
    t_bd_ev = dve(lambda e: e.tensor_copy(bd_s[:, :], P[0].ap[:, 0:NFC]),
                  waits=[last], reads=[P[0]])
    t_bp = dma("sync", "sq", brow, b_proj, waits=[t_bd_ev])
    for nb in range(4):
        tk = mm(P[1], P[1].ap, onesr_s[0:1, :], brow[:, nb * 512:(nb + 1) * 512],
                True, True, waits=[t_bp, t_or], inc="pe")
        dve(lambda e, nb=nb: e.tensor_copy(bproj_bc[:, nb * 512:(nb + 1) * 512],
                                           P[1].ap), waits=[tk], reads=[P[1]])
    t_bl = dma("sync", "sq", brow[:, 0:128], b_lat, waits=[P[1].war])
    tk = mm(P[0], P[0].ap[:, 16:17], brow[:, 0:128], ones11, True, True,
            waits=[t_bl], inc="pe")
    t_bl_ev = dve(lambda e: e.tensor_copy(blat_s[:, :], P[0].ap[:, 16:17]),
                  waits=[tk], reads=[P[0]])
    setup_tok = maxtok(t_bl_ev, t_id, t_tri, t_oc, t_wlat, t_rw)

    prev_xtoks = None
    prev_out_store = None
    prev_q_done = None

    inv_c = 1.0 / C

    prev_dump = [None]

    def ln_stats(src_f32, waits, per_ts=None):
        rstd_toks = []
        for ts in range(NTS):
            s = src_f32[:, ts * C:(ts + 1) * C]
            w_ts = list(waits) + ([per_ts[ts]] if per_ts else [])
            tsum = dve(lambda e, s=s, ts=ts: e.tensor_scalar(
                s, s, 1.0, None, ALU.mult, ALU.add,
                accum_out=stats[:, ts:ts + 1]), waits=w_ts)
            tsq = act(sqdump, s, AF.Square,
                      accum=stats[:, 4 + ts:5 + ts],
                      waits=w_ts + [tsum, prev_dump[0]])
            prev_dump[0] = tsq
            t1 = dve(lambda e, ts=ts: e.tensor_scalar(
                stats[:, 8 + ts:9 + ts], stats[:, ts:ts + 1],
                inv_c, None, ALU.mult, ALU.bypass), waits=[tsum])
            t2 = dve(lambda e, ts=ts: e.tensor_scalar(
                stats[:, 16 + ts:17 + ts], stats[:, 4 + ts:5 + ts],
                inv_c, EPS, ALU.mult, ALU.add), waits=[tsq])
            t3 = dve(lambda e, ts=ts: e.scalar_tensor_tensor(
                stats[:, 20 + ts:21 + ts], stats[:, 8 + ts:9 + ts], -1.0,
                stats[:, 8 + ts:9 + ts], ALU.mult, ALU.mult), waits=[t1])
            t4 = dve(lambda e, ts=ts: e.tensor_tensor(
                stats[:, 16 + ts:17 + ts], stats[:, 16 + ts:17 + ts],
                stats[:, 20 + ts:21 + ts], ALU.add), waits=[t2, t3])
            t5 = act(stats[:, 20 + ts:21 + ts], stats[:, 16 + ts:17 + ts],
                     AF.Ln, waits=[t4])
            t6 = act(stats[:, 12 + ts:13 + ts], stats[:, 20 + ts:21 + ts],
                     AF.Exp, scale=-0.5, waits=[t5])
            rstd_toks.append(t6)
        return rstd_toks

    def ln_apply_tp(rstd_toks, extra_war):
        evs = []
        tp_last = {}
        for ts in range(NTS):
            dst = xntok[ts % 2]
            ta = dve(lambda e, ts=ts, d=dst: e.tensor_scalar(
                d, sx[:, ts * C:(ts + 1) * C],
                stats[:, 8 + ts:9 + ts], stats[:, 12 + ts:13 + ts],
                ALU.subtract, ALU.mult),
                waits=maxtok(rstd_toks[ts],
                             extra_war if ts < 2 else tp_last.get(ts - 2)))
            tt = None
            for fc in range(NFC):
                i = ts * NFC + fc
                bank = PT[i % 2]
                tt = tpose(bank, dst[:, fc * 128:(fc + 1) * 128], waits=[ta])
                dst_fm = xnfm[:, fc * PANEL + ts * 128: fc * PANEL + (ts + 1) * 128]
                if i % 2 == 0:
                    ev = act(dst_fm, bank.ap, AF.Copy, waits=[tt], reads=[bank])
                else:
                    ev = dve(lambda e, d=dst_fm, b=bank: e.tensor_copy(d, b.ap),
                             waits=[tt], reads=[bank])
                evs.append(ev)
            tp_last[ts] = tt
        return maxtok(*evs[-4:])

    for p in range(NPANEL):
        seq, lp = divmod(p, LPN)
        t0 = lp * PANEL
        for st in streams:
            st.reset()

        if p == 0:
            xsrc = x_in[seq, t0:t0 + PANEL, :].rearrange("(ts q) c -> q ts c", q=128)
            t_x = dma("scalar", "xw", sx[:, :].rearrange("q (ts c) -> q ts c", c=C),
                      xsrc)
            prev_xtoks = [t_x] * NTS

        t_rstd1 = ln_stats(sx, maxtok(*setup_tok), per_ts=prev_xtoks)
        xn_ready = ln_apply_tp(t_rstd1, prev_q_done)

        last = None
        for fc in range(NFC):
            last = mm(P[4] if fc == 0 else None, P[4].ap,
                      wlat_win[:, fc * L:(fc + 1) * L],
                      xnfm[:, fc * PANEL:(fc + 1) * PANEL],
                      start=(fc == 0), stop=(fc == NFC - 1),
                      waits=maxtok(t_wlat, *xn_ready) if fc == 0 else (),
                      inc="pe" if fc == NFC - 1 else None)
        t_kev = act(klat_s[:, t0:t0 + PANEL], P[4].ap, AF.Identity,
                    bias=blat_s[:, :], waits=[last], reads=[P[4]])
        kt_evs = []
        for j in range(4):
            js = lp * 4 + j
            bank = PT[j % 2]
            tt = tpose(bank, klat_s[:, t0 + j * 128:t0 + (j + 1) * 128],
                       waits=[t_kev])
            ev = dve(lambda e, js=js, b=bank: e.tensor_copy(
                ktok_s[:, js * 128:(js + 1) * 128], b.ap), waits=[tt], reads=[bank])
            kt_evs.append(ev)
        ktok_ready = maxtok(*kt_evs)

        q_evs = []
        wd_st.ensure(0, p)
        for w in range(8):
            wd_st.ensure(w, p)
            wd_st.ensure(w + 1, p)
            buf, wtok = wd_st.win(w)
            last_mm = None
            for qcl in range(2):
                qc = w * 2 + qcl
                bank = P[qc % 3]
                for fc in range(NFC):
                    last_mm = mm(bank if fc == 0 else None, bank.ap,
                                 buf[:, fc * 256 + qcl * 128: fc * 256 + (qcl + 1) * 128],
                                 xnfm[:, fc * PANEL:(fc + 1) * PANEL],
                                 start=(fc == 0), stop=(fc == NFC - 1),
                                 waits=maxtok(wtok, *xn_ready) if fc == 0 else (),
                                 inc="pe" if fc == NFC - 1 else None)
                ev = act(Q[:, qc * 512:(qc + 1) * 512], bank.ap, AF.Identity,
                         bias=bd_s[:, qc:qc + 1], waits=[last_mm], reads=[bank])
                q_evs.append(ev)
            wd_st.consume(w, last_mm)
        q_ready = maxtok(*q_evs)

        njs = lp * 4 + 4
        exp_toks = {}
        avsum_toks = {}
        y_mult_toks = {}
        rec_tok = {}
        for h in range(NH):
            qh = Q[:, h * 512:(h + 1) * 512]
            ybank = P[4 + (h % 2)]
            sumbank = P[h % 2]
            smms = {}

            def emit_S(js, h=h, qh=qh):
                diag = js >= lp * 4
                jj = js - lp * 4
                c0 = jj * 128 if diag else 0
                sbank = P[2 + (js % 2)]
                smms[js] = mm(sbank, sbank.ap[:, c0:512],
                              klat_s[:, js * 128:(js + 1) * 128], qh[:, c0:512],
                              True, True,
                              waits=maxtok(*q_ready, *ktok_ready)
                              if (h == 0 and js == 0) else (),
                              inc="pe")

            emit_S(0)
            for js in range(njs):
                if js + 1 < njs:
                    emit_S(js + 1)
                diag = js >= lp * 4
                jj = js - lp * 4
                c0 = jj * 128 if diag else 0
                sbank = P[2 + (js % 2)]
                pb = PB[js % 2]
                war_av = avsum_toks.get((h, js - 2))
                if war_av is None and js < 2:
                    war_av = avsum_toks.get((h - 1, njs - 2 + js))
                te = act(pb[:, c0:512], sbank.ap[:, c0:512], AF.Exp, scale=SCALE,
                         waits=maxtok(smms[js], war_av), reads=[sbank])
                if diag:
                    te = dve(lambda e, pb=pb, c0=c0: e.tensor_tensor(
                        pb[:, c0:c0 + 128], pb[:, c0:c0 + 128], tri_s[:, :],
                        ALU.mult), waits=[te])
                exp_toks[(h, js)] = te
                av = mm(ybank if js == 0 else None, ybank.ap[:, c0:512],
                        ktok_s[:, js * 128:(js + 1) * 128], pb[:, c0:512],
                        start=(js == 0), stop=(js == njs - 1),
                        waits=[te], inc=None)
                sm = mm(sumbank if js == 0 else None,
                        sumbank.ap[0:1, c0:512], onesc_s[:, :], pb[:, c0:512],
                        start=(js == 0), stop=(js == njs - 1),
                        waits=(), inc="pe")
                avsum_toks[(h, js)] = sm
                if js == 0 and h >= 1:
                    sbp = P[(h - 1) % 2]
                    rec_tok[h - 1] = dve(
                        lambda e, sbp=sbp: e.reciprocal(rrow_s[:, :],
                                                        sbp.ap[0:1, :]),
                        waits=[avsum_toks[(h - 1, njs - 1)]], reads=[sbp])

            def emit_tail(hh, sb, yb):
                t_rec = rec_tok.get(hh)
                if t_rec is None:
                    t_rec = dve(lambda e, sb=sb: e.reciprocal(rrow_s[:, :],
                                                              sb.ap[0:1, :]),
                                waits=[avsum_toks[(hh, njs - 1)]], reads=[sb])
                t_rb = mm(sb, sb.ap, onesr_s[0:1, :], rrow_s[:, :],
                          True, True, waits=[t_rec], inc="pe")
                rbb = RBB[hh % 2]
                t_rbe = act(rbb, sb.ap, AF.Copy, waits=[t_rb], reads=[sb])
                t_ym = dve(lambda e, hh=hh, yb=yb, rb=rbb: e.tensor_tensor(
                    Y[:, hh * 512:(hh + 1) * 512], yb.ap, rb, ALU.mult),
                    waits=maxtok(t_rbe, avsum_toks[(hh, njs - 1)]), reads=[yb])
                y_mult_toks[hh] = t_ym

            if h >= 1:
                emit_tail(h - 1, P[(h - 1) % 2], P[4 + ((h - 1) % 2)])
        emit_tail(NH - 1, P[(NH - 1) % 2], P[4 + ((NH - 1) % 2)])
        attn_done = maxtok(*[y_mult_toks[h] for h in range(NH)])
        q_done = ("pe", S.cnt["pe"])

        proj_evs = []
        xn_free = ("pe", S.cnt["pe"])
        qwar[4].append(xn_free)
        qwar[5].append(xn_free)
        wp_st.ensure(0, p)
        for w in range(8):
            wp_st.ensure(w, p)
            wp_st.ensure(w + 1, p)
            wp_st.ensure(w + 2, p)
            buf, wtok = wp_st.win(w)
            last_for_slot = None
            for ts in range(NTS):
                i = w * NTS + ts
                bank = P[i % 3]
                last_mm = None
                for fc in range(NFC):
                    last_mm = mm(bank if fc == 0 else None, bank.ap[:, 0:256],
                                 Y[:, fc * 512 + ts * 128: fc * 512 + (ts + 1) * 128],
                                 buf[:, fc * 256:(fc + 1) * 256],
                                 start=(fc == 0), stop=(fc == NFC - 1),
                                 waits=maxtok(wtok, *(attn_done if i == 0 else []))
                                 if fc == 0 else (),
                                 inc="pe" if fc == NFC - 1 else None)
                ev = dve(lambda e, ts=ts, w=w, b=bank: e.tensor_tensor(
                    aof[:, ts * C + w * 256: ts * C + (w + 1) * 256],
                    b.ap[:, 0:256], bproj_bc[:, w * 256:(w + 1) * 256], ALU.add),
                    waits=maxtok(last_mm, prev_out_store), reads=[bank])
                proj_evs.append(ev)
                last_for_slot = last_mm
            wp_st.consume(w, last_for_slot)
        proj_done = maxtok(*proj_evs)

        t_rstd2 = ln_stats(aof, [], per_ts=proj_evs[-NTS:])
        x1_toks = []
        for ts in range(NTS):
            tz = dve(lambda e, ts=ts: e.tensor_scalar(
                aof[:, ts * C:(ts + 1) * C], aof[:, ts * C:(ts + 1) * C],
                stats[:, 8 + ts:9 + ts], stats[:, 12 + ts:13 + ts],
                ALU.subtract, ALU.mult), waits=[t_rstd2[ts]])
            x1_toks.append(dve(lambda e, ts=ts: e.tensor_tensor(
                sx[:, ts * C:(ts + 1) * C], sx[:, ts * C:(ts + 1) * C],
                aof[:, ts * C:(ts + 1) * C], ALU.add), waits=[tz]))

        t_rstd3 = ln_stats(sx, maxtok(*x1_toks))
        xn1_ready = ln_apply_tp(t_rstd3, q_done)

        rmm = None
        for ts in range(NTS):
            for fc in range(NFC):
                rmm = mm(P[3] if (ts == 0 and fc == 0) else None,
                         P[3].ap[:, ts * NE:(ts + 1) * NE],
                         xnfm[:, fc * PANEL + ts * 128: fc * PANEL + (ts + 1) * 128],
                         rw_win[:, fc * NE:(fc + 1) * NE],
                         start=(fc == 0), stop=(fc == NFC - 1),
                         waits=maxtok(t_rw, *xn1_ready) if (ts == 0 and fc == 0) else (),
                         inc="pe" if (ts == NTS - 1 and fc == NFC - 1) else None)
        t_lg = dve(lambda e: e.tensor_copy(logits_s[:, :], P[3].ap[:, 0:NTS * NE]),
                   waits=[rmm], reads=[P[3]])
        comb_toks = []
        for ts in range(NTS):
            lg = logits_s[:, ts * NE:(ts + 1) * NE]
            sc = scrf[:, 1024 + ts * NE:1024 + (ts + 1) * NE]
            tm1 = dve(lambda e, lg=lg, ts=ts: e.reduce_max(
                stats2[:, ts:ts + 1], lg, axis=AX.X), waits=[t_lg])
            teq = dve(lambda e, lg=lg, sc=sc, ts=ts: e.tensor_scalar(
                sc, lg, stats2[:, ts:ts + 1], -1e30, ALU.is_equal, ALU.mult),
                waits=[tm1])
            tl2 = dve(lambda e, lg=lg, sc=sc: e.tensor_tensor(
                sc, sc, lg, ALU.add), waits=[teq])
            tm2 = dve(lambda e, sc=sc, ts=ts: e.reduce_max(
                stats2[:, 4 + ts:5 + ts], sc, axis=AX.X), waits=[tl2])
            tneg = dve(lambda e, ts=ts: e.tensor_scalar(
                stats2[:, 8 + ts:9 + ts], stats2[:, ts:ts + 1], -1.0, None,
                ALU.mult, ALU.bypass), waits=[tm1])
            te2 = act(stats2[:, 12 + ts:13 + ts], stats2[:, 4 + ts:5 + ts],
                      AF.Exp, bias=stats2[:, 8 + ts:9 + ts],
                      waits=maxtok(tm2, tneg))
            ts1 = dve(lambda e, ts=ts: e.tensor_scalar(
                stats2[:, ts:ts + 1], stats2[:, 12 + ts:13 + ts], 1.0, None,
                ALU.add, ALU.bypass), waits=[te2])
            trec = dve(lambda e, ts=ts: e.reciprocal(
                stats2[:, 4 + ts:5 + ts], stats2[:, ts:ts + 1]), waits=[ts1])
            comb_toks.append(dve(lambda e, ts=ts: e.scalar_tensor_tensor(
                comb_s[:, ts:ts + 1], stats2[:, 12 + ts:13 + ts],
                stats2[:, 4 + ts:5 + ts], stats2[:, 4 + ts:5 + ts],
                ALU.mult, ALU.add), waits=[trec]))
        comb_done = maxtok(*comb_toks)

        silu_toks = {}
        hmul_toks = {}
        gu_st.ensure(0, p)
        for ic in range(NIC):
            w = ic
            gu_st.ensure(w, p)
            gu_st.ensure(w + 1, p)
            cbuf, gtok = gu_st.win(w)
            gbuf, ubuf = cbuf[:, 0:2048], cbuf[:, 2048:4096]
            utok = gtok
            gbank = P[2 * (ic % 3)]
            ubank = P[2 * (ic % 3) + 1]
            gmm = umm = None
            for fc in range(NFC):
                gmm = mm(gbank if fc == 0 else None, gbank.ap,
                         gbuf[:, fc * 128:(fc + 1) * 128],
                         xnfm[:, fc * PANEL:(fc + 1) * PANEL],
                         start=(fc == 0), stop=(fc == NFC - 1),
                         waits=maxtok(gtok, *(xn1_ready if ic == 0 else []))
                         if fc == 0 else (),
                         inc="pe" if fc == NFC - 1 else None)
            for fc in range(NFC):
                umm = mm(ubank if fc == 0 else None, ubank.ap,
                         ubuf[:, fc * 128:(fc + 1) * 128],
                         xnfm[:, fc * PANEL:(fc + 1) * PANEL],
                         start=(fc == 0), stop=(fc == NFC - 1),
                         waits=[utok] if fc == 0 else (),
                         inc="pe" if fc == NFC - 1 else None)
            tsg = act(SG[ic % 2], gbank.ap, AF.Silu,
                      waits=maxtok(gmm, hmul_toks.get(ic - 2),
                                   *(comb_done if ic == 0 else [])),
                      reads=[gbank])
            silu_toks[ic] = tsg
            hmul_toks[ic] = dve(lambda e, ic=ic, ub=ubank, sg=SG[ic % 2]:
                                e.tensor_tensor(H[:, ic * 512:(ic + 1) * 512],
                                                sg, ub.ap, ALU.mult),
                                waits=maxtok(tsg, umm), reads=[ubank])
            gu_st.consume(w, umm)

        rsilu = {}
        rhmul = {}
        rgu_st.ensure(0, p)
        for ric in range(NRIC):
            w = ric
            rgu_st.ensure(w, p)
            rgu_st.ensure(w + 1, p)
            cbuf, gtok = rgu_st.win(w)
            gbuf, ubuf = cbuf[:, 0:2048], cbuf[:, 2048:4096]
            utok = gtok
            gbank = P[2 * (ric % 3)]
            ubank = P[2 * (ric % 3) + 1]
            gmm = umm = None
            for fc in range(NFC):
                gmm = mm(gbank if fc == 0 else None, gbank.ap,
                         gbuf[:, fc * 128:(fc + 1) * 128],
                         xnfm[:, fc * PANEL:(fc + 1) * PANEL],
                         start=(fc == 0), stop=(fc == NFC - 1),
                         waits=[gtok] if fc == 0 else (),
                         inc="pe" if fc == NFC - 1 else None)
            for fc in range(NFC):
                umm = mm(ubank if fc == 0 else None, ubank.ap,
                         ubuf[:, fc * 128:(fc + 1) * 128],
                         xnfm[:, fc * PANEL:(fc + 1) * PANEL],
                         start=(fc == 0), stop=(fc == NFC - 1),
                         waits=[utok] if fc == 0 else (),
                         inc="pe" if fc == NFC - 1 else None)
            tsg = act(SG[ric % 2], gbank.ap, AF.Silu,
                      waits=maxtok(gmm, rhmul.get(ric - 2),
                                   hmul_toks[NIC - 2 + (ric % 2)] if ric < 2
                                   else None),
                      reads=[gbank])
            rsilu[ric] = tsg
            rhmul[ric] = dve(lambda e, ric=ric, ub=ubank, sg=SG[ric % 2]:
                             e.tensor_tensor(hr_s[:, ric * 512:(ric + 1) * 512],
                                             sg, ub.ap, ALU.mult),
                             waits=maxtok(tsg, umm), reads=[ubank])
            rgu_st.consume(w, umm)
        h_ready = maxtok(hmul_toks[NIC - 2], hmul_toks[NIC - 1])
        hr_ready = maxtok(rhmul[NRIC - 2], rhmul[NRIC - 1])

        dn_evs = {}
        xn1_free = ("pe", S.cnt["pe"])
        qwar[4].append(xn1_free)
        qwar[5].append(xn1_free)
        sd_st.ensure(0, p)
        for nb in range(4):
            last_per_ts = {}
            for icq in range(8):
                w = nb * 8 + icq
                sd_st.ensure(w, p)
                sd_st.ensure(w + 1, p)
                sd_st.ensure(w + 2, p)
                buf, wtok = sd_st.win(w)
                last_mm = None
                for ts in range(NTS):
                    bank = P[ts]
                    for icl in range(8):
                        ic = icq * 8 + icl
                        first = (icq == 0 and icl == 0)
                        wl = ()
                        if icl == 0:
                            wl = maxtok(
                                wtok,
                                hmul_toks.get(min(icq * 8 + 7, NIC - 1)),
                                *(h_ready if first and ts == 0 and nb == 0 else []))
                        last_mm = mm(
                            bank if first else None, bank.ap,
                            H[:, ic * 512 + ts * 128: ic * 512 + (ts + 1) * 128],
                            buf[:, icl * 512:(icl + 1) * 512],
                            start=first, stop=(icq == 7 and icl == 7),
                            waits=wl,
                            inc="pe" if (icl == 7 and (icq == 7 or ts == NTS - 1))
                            else None)
                    if icq == 7:
                        last_per_ts[ts] = last_mm
                sd_st.consume(w, last_mm)
                assert last_mm is not None
            for ts in range(NTS):
                dn_evs[(nb, ts)] = dve(
                    lambda e, ts=ts, nb=nb: e.tensor_copy(
                        aof[:, ts * C + nb * 512: ts * C + (nb + 1) * 512],
                        P[ts].ap),
                    waits=[last_per_ts[ts]], reads=[P[ts]])

        rdn_evs = {}
        rd_st.ensure(0, p)
        for nb in range(4):
            rd_st.ensure(nb, p)
            rd_st.ensure(nb + 1, p)
            buf, wtok = rd_st.win(nb)
            last_mm = None
            for ts in range(NTS):
                bank = P[4 + (ts % 2)]
                for ric in range(NRIC):
                    last_mm = mm(
                        bank if ric == 0 else None, bank.ap,
                        hr_s[:, ric * 512 + ts * 128: ric * 512 + (ts + 1) * 128],
                        buf[:, ric * 512:(ric + 1) * 512],
                        start=(ric == 0), stop=(ric == NRIC - 1),
                        waits=maxtok(wtok,
                                     *(hr_ready if (ts == 0 and nb == 0) else []))
                        if ric == 0 else (),
                        inc="pe" if ric == NRIC - 1 else None)
                rdn_evs[(nb, ts)] = dve(
                    lambda e, ts=ts, nb=nb, b=bank: e.scalar_tensor_tensor(
                        aof[:, ts * C + nb * 512: ts * C + (nb + 1) * 512],
                        b.ap, comb_s[:, ts:ts + 1],
                        aof[:, ts * C + nb * 512: ts * C + (nb + 1) * 512],
                        ALU.mult, ALU.add),
                    waits=maxtok(last_mm, dn_evs[(nb, ts)]), reads=[bank])
            rd_st.consume(nb, last_mm)
        f_done = maxtok(*[rdn_evs[(nb, ts)] for nb in range(4) for ts in range(NTS)])

        t_rstd4 = ln_stats(aof, f_done,
                           per_ts=[rdn_evs[(3, ts)] for ts in range(NTS)])
        out_toks = []
        next_xtoks = []
        if p + 1 < NPANEL:
            seq2, lp2 = divmod(p + 1, LPN)
            t02 = lp2 * PANEL
        for ts in range(NTS):
            tz = dve(lambda e, ts=ts: e.tensor_scalar(
                aof[:, ts * C:(ts + 1) * C], aof[:, ts * C:(ts + 1) * C],
                stats[:, 8 + ts:9 + ts], stats[:, 12 + ts:13 + ts],
                ALU.subtract, ALU.mult), waits=[t_rstd4[ts]])
            tadd = dve(lambda e, ts=ts: e.tensor_tensor(
                aof[:, ts * C:(ts + 1) * C], aof[:, ts * C:(ts + 1) * C],
                sx[:, ts * C:(ts + 1) * C], ALU.add), waits=[tz])
            out_toks.append(dma(
                "scalar", "ow",
                out[seq, t0 + ts * 128:t0 + (ts + 1) * 128, :],
                aof[:, ts * C:(ts + 1) * C], waits=[tadd]))
            if p + 1 < NPANEL:
                next_xtoks.append(dma(
                    "scalar", "xw", sx[:, ts * C:(ts + 1) * C],
                    x_in[seq2, t02 + ts * 128:t02 + (ts + 1) * 128, :],
                    waits=[tadd]))
        prev_xtoks = next_xtoks
        prev_out_store = out_toks[-1]
        prev_q_done = q_done

    S.emit("sync", lambda e: e.sem_inc(S.sems["sq"], 16),
           waits=[("ow", S.cnt["ow"])])

    with nc.Block() as block:
        @block.sync
        def _(e):
            S.replay("sync", e)

        @block.vector
        def _(e):
            S.replay("vector", e)

        @block.scalar
        def _(e):
            S.replay("scalar", e)

        @block.tensor
        def _(e):
            S.replay("tensor", e)

        @block.gpsimd
        def _(e):
            S.replay("gpsimd", e)

    es.close()
    return nc


_NC_CACHE = None


def kernel(L0_attn_x, L0_attn_W_lat, L0_attn_b_lat, L0_attn_W_d, L0_attn_b_d,
           L0_attn_W_proj, L0_attn_b_proj, L0_ffn_router_weight, L0_ffn_sh_gate,
           L0_ffn_sh_up, L0_ffn_sh_down, L0_ffn_r_gate, L0_ffn_r_up, L0_ffn_r_down):
    global _NC_CACHE
    x = np.ascontiguousarray(np.asarray(L0_attn_x, dtype=np.float32))
    w_lat_a = np.ascontiguousarray(np.asarray(L0_attn_W_lat, dtype=np.float32))
    b_lat_a = np.asarray(L0_attn_b_lat, dtype=np.float32).reshape(1, L)
    w_d_a = np.ascontiguousarray(np.asarray(L0_attn_W_d, dtype=np.float32))
    b_d_a = np.asarray(L0_attn_b_d, dtype=np.float32).reshape(1, C)
    w_proj_a = np.ascontiguousarray(np.asarray(L0_attn_W_proj, dtype=np.float32))
    b_proj_a = np.asarray(L0_attn_b_proj, dtype=np.float32).reshape(1, C)
    rw_a = np.ascontiguousarray(np.asarray(L0_ffn_router_weight, dtype=np.float32).T)
    gw_a = np.ascontiguousarray(np.asarray(L0_ffn_sh_gate, dtype=np.float32).T)
    uw_a = np.ascontiguousarray(np.asarray(L0_ffn_sh_up, dtype=np.float32).T)
    sd_a = np.ascontiguousarray(np.asarray(L0_ffn_sh_down, dtype=np.float32).T)
    rg_a = np.ascontiguousarray(np.asarray(L0_ffn_r_gate, dtype=np.float32).T)
    ru_a = np.ascontiguousarray(np.asarray(L0_ffn_r_up, dtype=np.float32).T)
    rd_a = np.ascontiguousarray(np.asarray(L0_ffn_r_down, dtype=np.float32).T)

    ident = np.eye(128, dtype=ml_dtypes.bfloat16)
    ii, jj = np.meshgrid(np.arange(128), np.arange(128), indexing="ij")
    tri01 = (jj >= ii).astype(ml_dtypes.bfloat16)
    onesc = np.ones((128, 1), dtype=ml_dtypes.bfloat16)
    onesr = np.ones((1, 128), dtype=np.float32)

    if _NC_CACHE is None:
        _NC_CACHE = build_program()
    nc = _NC_CACHE

    in_maps = []
    for c in range(8):
        in_maps.append({
            "x_in": np.ascontiguousarray(x[2 * c:2 * c + 2]),
            "w_lat": w_lat_a, "b_lat": b_lat_a, "w_d": w_d_a, "b_d": b_d_a,
            "w_proj": w_proj_a, "b_proj": b_proj_a, "rw_t": rw_a,
            "gw_t": gw_a, "uw_t": uw_a, "sd_t": sd_a,
            "rg_t": rg_a, "ru_t": ru_a, "rd_t": rd_a,
            "ident_in": ident, "tri_in": tri01, "onesc_in": onesc,
            "onesr_in": onesr,
        })
    res = run_bass_kernel_spmd(nc, in_maps, core_ids=list(range(8)))
    outs = [np.asarray(res.results[c]["out"], dtype=np.float32) for c in range(8)]
    return np.concatenate(outs, axis=0)


# version 18
# speedup vs baseline: 1.0663x; 1.0663x over previous
import math
import numpy as np
import ml_dtypes

import concourse.bass as bass
import concourse.mybir as mybir
from concourse.bass_utils import run_bass_kernel_spmd

F32 = mybir.dt.float32
BF16 = mybir.dt.bfloat16
AF = mybir.ActivationFunctionType
ALU = mybir.AluOpType
AX = mybir.AxisListType

BC = 2
T = 2048
C = 2048
NH = 16
L = 128
INTER = 8192
RINTER = 1024
NE = 8
PANEL = 512
NTS = 4
LPN = T // PANEL
NPANEL = BC * LPN
NFC = C // 128
NIC = INTER // 128
NRIC = RINTER // 128
SCALE = 1.0 / math.sqrt(L)
EPS = 1e-5


DMA_SEMS = ({"xw", "ow", "sq", "sg", "wst"}
            | {f"L{s}{p}" for s in ("wd", "wp", "gw", "uw", "sd", "rg", "ru", "rd")
               for p in (0, 1, 2)}
            | {f"G{s}{p}" for s in ("wd", "wp", "gw", "uw", "sd", "rg", "ru", "rd")
               for p in (0, 1, 2)})


class Sched:
    def __init__(self):
        self.items = []
        self.cnt = {}
        self.wmax = {}
        self.sems = {}

    def add_sem(self, name, handle):
        self.sems[name] = handle
        self.cnt[name] = 0

    def emit(self, eng, fn, waits=(), inc=None, inc_by=1):
        need = []
        for tok in waits:
            if tok is None:
                continue
            s, v = tok
            if s in DMA_SEMS:
                v = self.cnt[s]
            key = (eng, s)
            if v > self.wmax.get(key, 0):
                need.append((s, v))
                self.wmax[key] = v
        tok = None
        if inc is not None:
            self.cnt[inc] += inc_by
            tok = (inc, self.cnt[inc])
        self.items.append((eng, fn, tuple(need), (inc, inc_by) if inc else None))
        return tok

    def replay(self, eng_name, eng_obj):
        for eng, fn, waits, inc in self.items:
            if eng != eng_name:
                continue
            for s, v in waits:
                eng_obj.wait_ge(self.sems[s], v)
            inst = fn(eng_obj)
            if inc is not None:
                inst.then_inc(self.sems[inc[0]], inc[1])


def maxtok(*toks):
    best = {}
    for t in toks:
        if t is None:
            continue
        if isinstance(t, list):
            for tt in t:
                if tt is None:
                    continue
                s, v = tt
                if v > best.get(s, 0):
                    best[s] = v
            continue
        s, v = t
        if v > best.get(s, 0):
            best[s] = v
    return [(s, v) for s, v in best.items()]


def build_program():
    nc = bass.Bass("TRN2", target_bir_lowering=False)

    x_in = nc.dram_tensor("x_in", [BC, T, C], F32, kind="ExternalInput").ap()
    w_lat = nc.dram_tensor("w_lat", [C, L], F32, kind="ExternalInput").ap()
    b_lat = nc.dram_tensor("b_lat", [1, L], F32, kind="ExternalInput").ap()
    w_d = nc.dram_tensor("w_d", [C, C], F32, kind="ExternalInput").ap()
    b_d = nc.dram_tensor("b_d", [1, C], F32, kind="ExternalInput").ap()
    w_proj = nc.dram_tensor("w_proj", [C, C], F32, kind="ExternalInput").ap()
    b_proj = nc.dram_tensor("b_proj", [1, C], F32, kind="ExternalInput").ap()
    rw_t = nc.dram_tensor("rw_t", [C, NE], F32, kind="ExternalInput").ap()
    gw_t = nc.dram_tensor("gw_t", [C, INTER], F32, kind="ExternalInput").ap()
    uw_t = nc.dram_tensor("uw_t", [C, INTER], F32, kind="ExternalInput").ap()
    sd_t = nc.dram_tensor("sd_t", [INTER, C], F32, kind="ExternalInput").ap()
    rg_t = nc.dram_tensor("rg_t", [C, RINTER], F32, kind="ExternalInput").ap()
    ru_t = nc.dram_tensor("ru_t", [C, RINTER], F32, kind="ExternalInput").ap()
    rd_t = nc.dram_tensor("rd_t", [RINTER, C], F32, kind="ExternalInput").ap()
    ident_in = nc.dram_tensor("ident_in", [128, 128], BF16, kind="ExternalInput").ap()
    tri_in = nc.dram_tensor("tri_in", [128, 128], BF16, kind="ExternalInput").ap()
    onesc_in = nc.dram_tensor("onesc_in", [128, 1], BF16, kind="ExternalInput").ap()
    onesr_in = nc.dram_tensor("onesr_in", [1, 128], F32, kind="ExternalInput").ap()

    out = nc.dram_tensor("out", [BC, T, C], F32, kind="ExternalOutput").ap()

    def scrt(name, nwin, els):
        return nc.dram_tensor(name, [nwin, 128, els], BF16, kind="Internal").ap()

    wd_scr = scrt("wd_scr", 8, 16 * 256)
    wp_scr = scrt("wp_scr", 8, 16 * 256)
    gu_scr = scrt("gu_scr", 64, 2 * 16 * 128)
    sd_scr = scrt("sd_scr", 32, 8 * 512)
    rgu_scr = scrt("rgu_scr", 8, 2 * 16 * 128)
    rd_scr = scrt("rd_scr", 4, 8 * 512)

    from contextlib import ExitStack
    es = ExitStack()

    def sb(name, shape, dt):
        return es.enter_context(nc.sbuf_tensor(name, shape, dt))

    wlat_win = sb("wlat_win", [128, NFC * L], BF16)
    rw_win = sb("rw_win", [128, NFC * NE], BF16)
    ident_s = sb("ident_s", [128, 128], BF16)
    tri_s = sb("tri_s", [128, 128], BF16)
    onesc_s = sb("onesc_s", [128, 1], BF16)
    onesr_s = sb("onesr_s", [1, 128], F32)
    blat_s = sb("blat_s", [128, 1], F32)
    bd_s = sb("bd_s", [128, NFC], F32)
    bproj_bc = sb("bproj_bc", [128, C], F32)
    klat_s = sb("klat_s", [128, T], BF16)
    ktok_s = sb("ktok_s", [128, T], BF16)
    stats = sb("stats", [128, 32], F32)
    stats2 = sb("stats2", [128, 16], F32)
    logits_s = sb("logits_s", [128, NTS * NE], F32)
    comb_s = sb("comb_s", [128, NTS], F32)
    sx = sb("sx", [128, NTS * C], F32)
    aof = sb("aof", [128, NTS * C], F32)
    xnfm = sb("xnfm", [128, NFC * PANEL], BF16)
    ovl = sb("ovl", [128, 32768], BF16)
    hr_s = sb("hr_s", [128, NRIC * PANEL], BF16)
    scrf = sb("scrf", [128, 1536], F32)
    warena = sb("warena", [128, 8192], BF16)

    Q = ovl[:, 0:8192]
    xntok = [ovl[:, 0:2048], ovl[:, 2048:4096]]
    Y = ovl[:, 8192:16384]
    PB = [ovl[:, 16384 + i * 512: 16384 + (i + 1) * 512] for i in range(2)]
    H = ovl[:, 0:32768]
    SG = [scrf[:, 0:512], scrf[:, 512:1024]]
    RBB = [scrf[:, 0:512], scrf[:, 512:1024]]
    rrow_s = scrf[0:1, 1024:1536]
    sqdump = ovl[:, 21504:23552]
    sqdump2 = ovl[:, 23552:25600]

    def psal(name, shape, dt):
        return es.enter_context(nc.psum_tensor(name, shape, dt))

    class PBank:
        def __init__(self, ap):
            self.ap = ap
            self.war = None

    P = [PBank(psal(f"p{i}", [128, 512], F32)[:, :]) for i in range(6)]
    PT = [PBank(psal("pt0", [128, 1024], BF16)[:, 0:128]),
          PBank(psal("pt1", [128, 1024], BF16)[:, 0:128])]

    S = Sched()
    for s in sorted(DMA_SEMS) + ["pe", "act", "dve"]:
        S.add_sem(s, es.enter_context(nc.semaphore(s)))

    def dma(eng, sem, out_ap, in_ap, waits=()):
        return S.emit(eng, lambda e, o=out_ap, i=in_ap: e.dma_start(out=o, in_=i),
                      waits=waits, inc=sem, inc_by=16)

    def mm(bank, out_ap, lhsT, rhs, start, stop, waits=(), inc=None):
        w = list(waits)
        if start and bank is not None:
            w.append(bank.war)
        return S.emit("tensor",
                      lambda e, o=out_ap, l=lhsT, r=rhs, st=start, sp=stop:
                      e.matmul(o, l, r, start=st, stop=sp),
                      waits=maxtok(*w), inc=inc)

    def tpose(bank, in_ap, waits=()):
        w = list(waits) + [bank.war]
        return S.emit("tensor",
                      lambda e, o=bank.ap, i=in_ap: e.transpose(o, i, ident_s[:, :]),
                      waits=maxtok(*w), inc="pe")

    def act(out_ap, in_ap, func, bias=None, scale=None, accum=None, waits=(),
            reads=()):
        def f(e, o=out_ap, i=in_ap, fu=func, b=bias, sc=scale, a=accum):
            kw = {}
            if b is not None:
                kw["bias"] = b
            if sc is not None:
                kw["scale"] = sc
            if a is not None:
                kw["accum_out"] = a
            return e.activation(o, i, fu, **kw)
        t = S.emit("scalar", f, waits=maxtok(*waits), inc="act")
        for b in reads:
            b.war = t
        return t

    def dve(fn, waits=(), reads=()):
        t = S.emit("vector", fn, waits=maxtok(*waits), inc="dve")
        for b in reads:
            b.war = t
        return t

    qwar = [[], [], [], [], [], []]
    qap = [warena[:, i * 2048:(i + 1) * 2048] for i in range(4)]
    dslot = {0: warena[:, 0:4096], 1: warena[:, 4096:8192],
             2: xnfm[:, 0:4096]}

    class WStream:
        def __init__(self, name, scratch, nwin, win_els, quarters, src_fn,
                     sems, src_fn2=None):
            self.name = name
            self.scratch = scratch
            self.nwin = nwin
            self.win_els = win_els
            self.quarters = quarters
            self.src_fn = src_fn
            self.src_fn2 = src_fn2
            self.sems = sems
            self.load_tok = [None] * len(quarters)
            self.store_tok = [None] * nwin
            self.fetched = -1

        def nbuf(self):
            return len(self.quarters)

        def buf_ap(self, w):
            qs = self.quarters[w % self.nbuf()]
            if len(qs) == 1:
                return qap[qs[0]][:, 0:self.win_els]
            return dslot[{0: 0, 2: 1, 4: 2}[qs[0]]][:, 0:self.win_els]

        def ensure(self, w, panel):
            if w >= self.nwin or w <= self.fetched:
                return
            self.fetched = w
            par = w % self.nbuf()
            qs = self.quarters[par]
            buf = self.buf_ap(w)
            wars = [t for q in qs for t in qwar[q]]
            if panel == 0:
                gsem = "G" + self.sems[par][1:]
                if self.src_fn2 is not None:
                    half = self.win_els // 2
                    dma("gpsimd", gsem, buf[:, 0:half], self.src_fn(w),
                        waits=maxtok(*wars))
                    t = dma("gpsimd", gsem, buf[:, half:2 * half],
                            self.src_fn2(w), waits=maxtok(*wars))
                else:
                    t = dma("gpsimd", gsem, buf, self.src_fn(w),
                            waits=maxtok(*wars))
                self.load_tok[par] = t
                st = dma("sync", "wst", self.scratch[w], buf, waits=[t])
                self.store_tok[w] = st
                for q in qs:
                    qwar[q] = [st]
            else:
                t = dma("gpsimd", "G" + self.sems[par][1:], buf, self.scratch[w],
                        waits=maxtok(*(wars + [self.store_tok[w]])))
                self.load_tok[par] = t
                for q in qs:
                    qwar[q] = []

        def win(self, w):
            return self.buf_ap(w), self.load_tok[w % self.nbuf()]

        def consume(self, w, tok):
            for q in self.quarters[w % self.nbuf()]:
                qwar[q].append(tok)

        def reset(self):
            self.fetched = -1

    def colwin(src, kc0, nkc, n0, wn):
        return src[kc0 * 128:(kc0 + nkc) * 128, n0:n0 + wn].rearrange(
            "(k p) n -> p k n", p=128)

    wd_st = WStream("wd", wd_scr, 8, 16 * 256, [(0, 1), (2, 3)],
                    lambda w: colwin(w_d, 0, 16, w * 256, 256),
                    sems=("Lwd0", "Lwd1"))
    wp_st = WStream("wp", wp_scr, 8, 16 * 256, [(0, 1), (2, 3), (4, 5)],
                    lambda w: colwin(w_proj, 0, 16, w * 256, 256),
                    sems=("Lwp0", "Lwp1", "Lwp2"))
    gu_st = WStream("gu", gu_scr, 64, 2 * 16 * 128, [(0, 1), (2, 3)],
                    lambda w: colwin(gw_t, 0, 16, w * 128, 128),
                    sems=("Lgw0", "Lgw1"),
                    src_fn2=lambda w: colwin(uw_t, 0, 16, w * 128, 128))
    sd_st = WStream("sd", sd_scr, 32, 8 * 512, [(0, 1), (2, 3), (4, 5)],
                    lambda w: colwin(sd_t, (w % 8) * 8, 8, (w // 8) * 512, 512),
                    sems=("Lsd0", "Lsd1", "Lsd2"))
    rgu_st = WStream("rgu", rgu_scr, 8, 2 * 16 * 128, [(0, 1), (2, 3)],
                     lambda w: colwin(rg_t, 0, 16, w * 128, 128),
                     sems=("Lrg0", "Lrg1"),
                     src_fn2=lambda w: colwin(ru_t, 0, 16, w * 128, 128))
    rd_st = WStream("rd", rd_scr, 4, 8 * 512, [(0, 1), (2, 3)],
                    lambda w: colwin(rd_t, 0, 8, w * 512, 512),
                    sems=("Lrd0", "Lrd1"))
    streams = [wd_st, wp_st, gu_st, sd_st, rgu_st, rd_st]

    t_id = dma("sync", "sq", ident_s[:, :], ident_in)
    t_tri = dma("sync", "sq", tri_s[:, :], tri_in)
    t_oc = dma("sync", "sq", onesc_s[:, :], onesc_in)
    t_or = dma("sync", "sq", onesr_s[:, :], onesr_in)
    t_wlat = dma("gpsimd", "sg", wlat_win[:, :].rearrange("p (k n) -> p k n", n=L),
                 colwin(w_lat, 0, 16, 0, L))
    t_rw = dma("gpsimd", "sg", rw_win[:, :].rearrange("p (k n) -> p k n", n=NE),
               colwin(rw_t, 0, 16, 0, NE))

    brow = aof[0:1, 0:2048]
    ones11 = onesr_s[0:1, 0:1]
    t_bd = dma("sync", "sq", brow, b_d, waits=[t_or])
    last = None
    for qc in range(NFC):
        last = mm(P[0] if qc == 0 else None, P[0].ap[:, qc:qc + 1],
                  brow[:, qc * 128:(qc + 1) * 128], ones11, True, True,
                  waits=[t_bd], inc="pe" if qc == NFC - 1 else None)
    t_bd_ev = dve(lambda e: e.tensor_copy(bd_s[:, :], P[0].ap[:, 0:NFC]),
                  waits=[last], reads=[P[0]])
    t_bp = dma("sync", "sq", brow, b_proj, waits=[t_bd_ev])
    for nb in range(4):
        tk = mm(P[1], P[1].ap, onesr_s[0:1, :], brow[:, nb * 512:(nb + 1) * 512],
                True, True, waits=[t_bp, t_or], inc="pe")
        dve(lambda e, nb=nb: e.tensor_copy(bproj_bc[:, nb * 512:(nb + 1) * 512],
                                           P[1].ap), waits=[tk], reads=[P[1]])
    t_bl = dma("sync", "sq", brow[:, 0:128], b_lat, waits=[P[1].war])
    tk = mm(P[0], P[0].ap[:, 16:17], brow[:, 0:128], ones11, True, True,
            waits=[t_bl], inc="pe")
    t_bl_ev = dve(lambda e: e.tensor_copy(blat_s[:, :], P[0].ap[:, 16:17]),
                  waits=[tk], reads=[P[0]])
    setup_tok = maxtok(t_bl_ev, t_id, t_tri, t_oc, t_wlat, t_rw)

    prev_xtoks = None
    prev_out_store = None
    prev_q_done = None

    inv_c = 1.0 / C

    prev_dump = [None]

    def ln_stats(src_f32, waits, per_ts=None):
        toks = []
        for ts in range(NTS):
            s = src_f32[:, ts * C:(ts + 1) * C]
            w_ts = list(waits) + ([per_ts[ts]] if per_ts else [])
            tsum = dve(lambda e, s=s, ts=ts: e.tensor_scalar(
                s, s, 1.0, None, ALU.mult, ALU.add,
                accum_out=stats[:, ts:ts + 1]), waits=w_ts)
            tsq = act(sqdump, s, AF.Square,
                      accum=stats[:, 4 + ts:5 + ts],
                      waits=w_ts + [tsum, prev_dump[0]])
            prev_dump[0] = tsq
            toks.append(tsum)
            toks.append(tsq)
        t1 = dve(lambda e: e.tensor_scalar(stats[:, 8:12], stats[:, 0:4],
                                           inv_c, None, ALU.mult, ALU.bypass),
                 waits=toks[-4:])
        t2 = dve(lambda e: e.tensor_scalar(stats[:, 16:20], stats[:, 4:8],
                                           inv_c, EPS, ALU.mult, ALU.add),
                 waits=toks[-4:])
        t3 = dve(lambda e: e.scalar_tensor_tensor(
            stats[:, 20:24], stats[:, 8:12], -1.0, stats[:, 8:12],
            ALU.mult, ALU.mult), waits=[t1])
        t4 = dve(lambda e: e.tensor_tensor(stats[:, 16:20], stats[:, 16:20],
                                           stats[:, 20:24], ALU.add),
                 waits=[t2, t3])
        t5 = act(stats[:, 20:24], stats[:, 16:20], AF.Ln, waits=[t4])
        t6 = act(stats[:, 12:16], stats[:, 20:24], AF.Exp, scale=-0.5, waits=[t5])
        return t6

    def ln_apply_tp(rstd_tok, extra_war):
        evs = []
        tp_last = {}
        for ts in range(NTS):
            dst = xntok[ts % 2]
            ta = dve(lambda e, ts=ts, d=dst: e.tensor_scalar(
                d, sx[:, ts * C:(ts + 1) * C],
                stats[:, 8 + ts:9 + ts], stats[:, 12 + ts:13 + ts],
                ALU.subtract, ALU.mult),
                waits=maxtok(rstd_tok,
                             extra_war if ts < 2 else tp_last.get(ts - 2)))
            tt = None
            for fc in range(NFC):
                i = ts * NFC + fc
                bank = PT[i % 2]
                tt = tpose(bank, dst[:, fc * 128:(fc + 1) * 128], waits=[ta])
                dst_fm = xnfm[:, fc * PANEL + ts * 128: fc * PANEL + (ts + 1) * 128]
                if i % 2 == 0:
                    ev = act(dst_fm, bank.ap, AF.Copy, waits=[tt], reads=[bank])
                else:
                    ev = dve(lambda e, d=dst_fm, b=bank: e.tensor_copy(d, b.ap),
                             waits=[tt], reads=[bank])
                evs.append(ev)
            tp_last[ts] = tt
        return maxtok(*evs[-4:])

    for p in range(NPANEL):
        seq, lp = divmod(p, LPN)
        t0 = lp * PANEL
        for st in streams:
            st.reset()

        if p == 0:
            xsrc = x_in[seq, t0:t0 + PANEL, :].rearrange("(ts q) c -> q ts c", q=128)
            t_x = dma("scalar", "xw", sx[:, :].rearrange("q (ts c) -> q ts c", c=C),
                      xsrc)
            prev_xtoks = [t_x] * NTS

        t_rstd1 = ln_stats(sx, maxtok(*setup_tok), per_ts=prev_xtoks)
        xn_ready = ln_apply_tp(t_rstd1, prev_q_done)

        last = None
        for fc in range(NFC):
            last = mm(P[4] if fc == 0 else None, P[4].ap,
                      wlat_win[:, fc * L:(fc + 1) * L],
                      xnfm[:, fc * PANEL:(fc + 1) * PANEL],
                      start=(fc == 0), stop=(fc == NFC - 1),
                      waits=maxtok(t_wlat, *xn_ready) if fc == 0 else (),
                      inc="pe" if fc == NFC - 1 else None)
        t_kev = act(klat_s[:, t0:t0 + PANEL], P[4].ap, AF.Identity,
                    bias=blat_s[:, :], waits=[last], reads=[P[4]])
        kt_evs = []
        for j in range(4):
            js = lp * 4 + j
            bank = PT[j % 2]
            tt = tpose(bank, klat_s[:, t0 + j * 128:t0 + (j + 1) * 128],
                       waits=[t_kev])
            ev = dve(lambda e, js=js, b=bank: e.tensor_copy(
                ktok_s[:, js * 128:(js + 1) * 128], b.ap), waits=[tt], reads=[bank])
            kt_evs.append(ev)
        ktok_ready = maxtok(*kt_evs)

        q_evs = []
        wd_st.ensure(0, p)
        for w in range(8):
            wd_st.ensure(w, p)
            wd_st.ensure(w + 1, p)
            buf, wtok = wd_st.win(w)
            last_mm = None
            for qcl in range(2):
                qc = w * 2 + qcl
                bank = P[qc % 3]
                for fc in range(NFC):
                    last_mm = mm(bank if fc == 0 else None, bank.ap,
                                 buf[:, fc * 256 + qcl * 128: fc * 256 + (qcl + 1) * 128],
                                 xnfm[:, fc * PANEL:(fc + 1) * PANEL],
                                 start=(fc == 0), stop=(fc == NFC - 1),
                                 waits=maxtok(wtok, *xn_ready) if fc == 0 else (),
                                 inc="pe" if fc == NFC - 1 else None)
                ev = act(Q[:, qc * 512:(qc + 1) * 512], bank.ap, AF.Identity,
                         bias=bd_s[:, qc:qc + 1], waits=[last_mm], reads=[bank])
                q_evs.append(ev)
            wd_st.consume(w, last_mm)
        q_ready = maxtok(*q_evs)

        njs = lp * 4 + 4
        exp_toks = {}
        avsum_toks = {}
        y_mult_toks = {}
        rec_tok = {}
        for h in range(NH):
            qh = Q[:, h * 512:(h + 1) * 512]
            ybank = P[4 + (h % 2)]
            sumbank = P[h % 2]
            smms = {}

            def emit_S(js, h=h, qh=qh):
                diag = js >= lp * 4
                jj = js - lp * 4
                c0 = jj * 128 if diag else 0
                sbank = P[2 + (js % 2)]
                smms[js] = mm(sbank, sbank.ap[:, c0:512],
                              klat_s[:, js * 128:(js + 1) * 128], qh[:, c0:512],
                              True, True,
                              waits=maxtok(*q_ready, *ktok_ready)
                              if (h == 0 and js == 0) else (),
                              inc="pe")

            emit_S(0)
            for js in range(njs):
                if js + 1 < njs:
                    emit_S(js + 1)
                diag = js >= lp * 4
                jj = js - lp * 4
                c0 = jj * 128 if diag else 0
                sbank = P[2 + (js % 2)]
                pb = PB[js % 2]
                war_av = avsum_toks.get((h, js - 2))
                if war_av is None and js < 2:
                    war_av = avsum_toks.get((h - 1, njs - 2 + js))
                te = act(pb[:, c0:512], sbank.ap[:, c0:512], AF.Exp, scale=SCALE,
                         waits=maxtok(smms[js], war_av), reads=[sbank])
                if diag:
                    te = dve(lambda e, pb=pb, c0=c0: e.tensor_tensor(
                        pb[:, c0:c0 + 128], pb[:, c0:c0 + 128], tri_s[:, :],
                        ALU.mult), waits=[te])
                exp_toks[(h, js)] = te
                av = mm(ybank if js == 0 else None, ybank.ap[:, c0:512],
                        ktok_s[:, js * 128:(js + 1) * 128], pb[:, c0:512],
                        start=(js == 0), stop=(js == njs - 1),
                        waits=[te], inc=None)
                sm = mm(sumbank if js == 0 else None,
                        sumbank.ap[0:1, c0:512], onesc_s[:, :], pb[:, c0:512],
                        start=(js == 0), stop=(js == njs - 1),
                        waits=(), inc="pe")
                avsum_toks[(h, js)] = sm
                if js == 0 and h >= 1:
                    sbp = P[(h - 1) % 2]
                    rec_tok[h - 1] = dve(
                        lambda e, sbp=sbp: e.reciprocal(rrow_s[:, :],
                                                        sbp.ap[0:1, :]),
                        waits=[avsum_toks[(h - 1, njs - 1)]], reads=[sbp])

            def emit_tail(hh, sb, yb):
                t_rec = rec_tok.get(hh)
                if t_rec is None:
                    t_rec = dve(lambda e, sb=sb: e.reciprocal(rrow_s[:, :],
                                                              sb.ap[0:1, :]),
                                waits=[avsum_toks[(hh, njs - 1)]], reads=[sb])
                t_rb = mm(sb, sb.ap, onesr_s[0:1, :], rrow_s[:, :],
                          True, True, waits=[t_rec], inc="pe")
                rbb = RBB[hh % 2]
                t_rbe = act(rbb, sb.ap, AF.Copy, waits=[t_rb], reads=[sb])
                t_ym = dve(lambda e, hh=hh, yb=yb, rb=rbb: e.tensor_tensor(
                    Y[:, hh * 512:(hh + 1) * 512], yb.ap, rb, ALU.mult),
                    waits=maxtok(t_rbe, avsum_toks[(hh, njs - 1)]), reads=[yb])
                y_mult_toks[hh] = t_ym

            if h >= 1:
                emit_tail(h - 1, P[(h - 1) % 2], P[4 + ((h - 1) % 2)])
        emit_tail(NH - 1, P[(NH - 1) % 2], P[4 + ((NH - 1) % 2)])
        attn_done = maxtok(*[y_mult_toks[h] for h in range(NH)])
        q_done = ("pe", S.cnt["pe"])

        proj_evs = []
        xn_free = ("pe", S.cnt["pe"])
        qwar[4].append(xn_free)
        qwar[5].append(xn_free)
        wp_st.ensure(0, p)
        for w in range(8):
            wp_st.ensure(w, p)
            wp_st.ensure(w + 1, p)
            wp_st.ensure(w + 2, p)
            buf, wtok = wp_st.win(w)
            last_for_slot = None
            for ts in range(NTS):
                i = w * NTS + ts
                bank = P[i % 3]
                last_mm = None
                for fc in range(NFC):
                    last_mm = mm(bank if fc == 0 else None, bank.ap[:, 0:256],
                                 Y[:, fc * 512 + ts * 128: fc * 512 + (ts + 1) * 128],
                                 buf[:, fc * 256:(fc + 1) * 256],
                                 start=(fc == 0), stop=(fc == NFC - 1),
                                 waits=maxtok(wtok, *(attn_done if i == 0 else []))
                                 if fc == 0 else (),
                                 inc="pe" if fc == NFC - 1 else None)
                ev = dve(lambda e, ts=ts, w=w, b=bank: e.tensor_tensor(
                    aof[:, ts * C + w * 256: ts * C + (w + 1) * 256],
                    b.ap[:, 0:256], bproj_bc[:, w * 256:(w + 1) * 256], ALU.add),
                    waits=maxtok(last_mm, prev_out_store), reads=[bank])
                proj_evs.append(ev)
                last_for_slot = last_mm
            wp_st.consume(w, last_for_slot)
        proj_done = maxtok(*proj_evs)

        t_rstd2 = ln_stats(aof, [], per_ts=proj_evs[-NTS:])
        x1_toks = []
        for ts in range(NTS):
            tz = dve(lambda e, ts=ts: e.tensor_scalar(
                aof[:, ts * C:(ts + 1) * C], aof[:, ts * C:(ts + 1) * C],
                stats[:, 8 + ts:9 + ts], stats[:, 12 + ts:13 + ts],
                ALU.subtract, ALU.mult), waits=[t_rstd2])
            x1_toks.append(dve(lambda e, ts=ts: e.tensor_tensor(
                sx[:, ts * C:(ts + 1) * C], sx[:, ts * C:(ts + 1) * C],
                aof[:, ts * C:(ts + 1) * C], ALU.add), waits=[tz]))

        t_rstd3 = ln_stats(sx, maxtok(*x1_toks))
        xn1_ready = ln_apply_tp(t_rstd3, q_done)

        rmm = None
        for ts in range(NTS):
            for fc in range(NFC):
                rmm = mm(P[3] if (ts == 0 and fc == 0) else None,
                         P[3].ap[:, ts * NE:(ts + 1) * NE],
                         xnfm[:, fc * PANEL + ts * 128: fc * PANEL + (ts + 1) * 128],
                         rw_win[:, fc * NE:(fc + 1) * NE],
                         start=(fc == 0), stop=(fc == NFC - 1),
                         waits=maxtok(t_rw, *xn1_ready) if (ts == 0 and fc == 0) else (),
                         inc="pe" if (ts == NTS - 1 and fc == NFC - 1) else None)
        t_lg = dve(lambda e: e.tensor_copy(logits_s[:, :], P[3].ap[:, 0:NTS * NE]),
                   waits=[rmm], reads=[P[3]])
        comb_toks = []
        for ts in range(NTS):
            lg = logits_s[:, ts * NE:(ts + 1) * NE]
            sc = scrf[:, 1024 + ts * NE:1024 + (ts + 1) * NE]
            tm1 = dve(lambda e, lg=lg, ts=ts: e.reduce_max(
                stats2[:, ts:ts + 1], lg, axis=AX.X), waits=[t_lg])
            teq = dve(lambda e, lg=lg, sc=sc, ts=ts: e.tensor_scalar(
                sc, lg, stats2[:, ts:ts + 1], -1e30, ALU.is_equal, ALU.mult),
                waits=[tm1])
            tl2 = dve(lambda e, lg=lg, sc=sc: e.tensor_tensor(
                sc, sc, lg, ALU.add), waits=[teq])
            tm2 = dve(lambda e, sc=sc, ts=ts: e.reduce_max(
                stats2[:, 4 + ts:5 + ts], sc, axis=AX.X), waits=[tl2])
            tneg = dve(lambda e, ts=ts: e.tensor_scalar(
                stats2[:, 8 + ts:9 + ts], stats2[:, ts:ts + 1], -1.0, None,
                ALU.mult, ALU.bypass), waits=[tm1])
            te2 = act(stats2[:, 12 + ts:13 + ts], stats2[:, 4 + ts:5 + ts],
                      AF.Exp, bias=stats2[:, 8 + ts:9 + ts],
                      waits=maxtok(tm2, tneg))
            ts1 = dve(lambda e, ts=ts: e.tensor_scalar(
                stats2[:, ts:ts + 1], stats2[:, 12 + ts:13 + ts], 1.0, None,
                ALU.add, ALU.bypass), waits=[te2])
            trec = dve(lambda e, ts=ts: e.reciprocal(
                stats2[:, 4 + ts:5 + ts], stats2[:, ts:ts + 1]), waits=[ts1])
            comb_toks.append(dve(lambda e, ts=ts: e.scalar_tensor_tensor(
                comb_s[:, ts:ts + 1], stats2[:, 12 + ts:13 + ts],
                stats2[:, 4 + ts:5 + ts], stats2[:, 4 + ts:5 + ts],
                ALU.mult, ALU.add), waits=[trec]))
        comb_done = maxtok(*comb_toks)

        silu_toks = {}
        hmul_toks = {}
        gu_st.ensure(0, p)
        for ic in range(NIC):
            w = ic
            gu_st.ensure(w, p)
            gu_st.ensure(w + 1, p)
            cbuf, gtok = gu_st.win(w)
            gbuf, ubuf = cbuf[:, 0:2048], cbuf[:, 2048:4096]
            utok = gtok
            gbank = P[2 * (ic % 3)]
            ubank = P[2 * (ic % 3) + 1]
            gmm = umm = None
            for fc in range(NFC):
                gmm = mm(gbank if fc == 0 else None, gbank.ap,
                         gbuf[:, fc * 128:(fc + 1) * 128],
                         xnfm[:, fc * PANEL:(fc + 1) * PANEL],
                         start=(fc == 0), stop=(fc == NFC - 1),
                         waits=maxtok(gtok, *(xn1_ready if ic == 0 else []))
                         if fc == 0 else (),
                         inc="pe" if fc == NFC - 1 else None)
            for fc in range(NFC):
                umm = mm(ubank if fc == 0 else None, ubank.ap,
                         ubuf[:, fc * 128:(fc + 1) * 128],
                         xnfm[:, fc * PANEL:(fc + 1) * PANEL],
                         start=(fc == 0), stop=(fc == NFC - 1),
                         waits=[utok] if fc == 0 else (),
                         inc="pe" if fc == NFC - 1 else None)
            tsg = act(SG[ic % 2], gbank.ap, AF.Silu,
                      waits=maxtok(gmm, hmul_toks.get(ic - 2),
                                   *(comb_done if ic == 0 else [])),
                      reads=[gbank])
            silu_toks[ic] = tsg
            hmul_toks[ic] = dve(lambda e, ic=ic, ub=ubank, sg=SG[ic % 2]:
                                e.tensor_tensor(H[:, ic * 512:(ic + 1) * 512],
                                                sg, ub.ap, ALU.mult),
                                waits=maxtok(tsg, umm), reads=[ubank])
            gu_st.consume(w, umm)

        rsilu = {}
        rhmul = {}
        rgu_st.ensure(0, p)
        for ric in range(NRIC):
            w = ric
            rgu_st.ensure(w, p)
            rgu_st.ensure(w + 1, p)
            cbuf, gtok = rgu_st.win(w)
            gbuf, ubuf = cbuf[:, 0:2048], cbuf[:, 2048:4096]
            utok = gtok
            gbank = P[2 * (ric % 3)]
            ubank = P[2 * (ric % 3) + 1]
            gmm = umm = None
            for fc in range(NFC):
                gmm = mm(gbank if fc == 0 else None, gbank.ap,
                         gbuf[:, fc * 128:(fc + 1) * 128],
                         xnfm[:, fc * PANEL:(fc + 1) * PANEL],
                         start=(fc == 0), stop=(fc == NFC - 1),
                         waits=[gtok] if fc == 0 else (),
                         inc="pe" if fc == NFC - 1 else None)
            for fc in range(NFC):
                umm = mm(ubank if fc == 0 else None, ubank.ap,
                         ubuf[:, fc * 128:(fc + 1) * 128],
                         xnfm[:, fc * PANEL:(fc + 1) * PANEL],
                         start=(fc == 0), stop=(fc == NFC - 1),
                         waits=[utok] if fc == 0 else (),
                         inc="pe" if fc == NFC - 1 else None)
            tsg = act(SG[ric % 2], gbank.ap, AF.Silu,
                      waits=maxtok(gmm, rhmul.get(ric - 2),
                                   hmul_toks[NIC - 2 + (ric % 2)] if ric < 2
                                   else None),
                      reads=[gbank])
            rsilu[ric] = tsg
            rhmul[ric] = dve(lambda e, ric=ric, ub=ubank, sg=SG[ric % 2]:
                             e.tensor_tensor(hr_s[:, ric * 512:(ric + 1) * 512],
                                             sg, ub.ap, ALU.mult),
                             waits=maxtok(tsg, umm), reads=[ubank])
            rgu_st.consume(w, umm)
        h_ready = maxtok(hmul_toks[NIC - 2], hmul_toks[NIC - 1])
        hr_ready = maxtok(rhmul[NRIC - 2], rhmul[NRIC - 1])

        dn_evs = {}
        xn1_free = ("pe", S.cnt["pe"])
        qwar[4].append(xn1_free)
        qwar[5].append(xn1_free)
        sd_st.ensure(0, p)
        for nb in range(4):
            last_per_ts = {}
            for icq in range(8):
                w = nb * 8 + icq
                sd_st.ensure(w, p)
                sd_st.ensure(w + 1, p)
                sd_st.ensure(w + 2, p)
                buf, wtok = sd_st.win(w)
                last_mm = None
                for ts in range(NTS):
                    bank = P[ts]
                    for icl in range(8):
                        ic = icq * 8 + icl
                        first = (icq == 0 and icl == 0)
                        wl = ()
                        if icl == 0:
                            wl = maxtok(
                                wtok,
                                hmul_toks.get(min(icq * 8 + 7, NIC - 1)),
                                *(h_ready if first and ts == 0 and nb == 0 else []))
                        last_mm = mm(
                            bank if first else None, bank.ap,
                            H[:, ic * 512 + ts * 128: ic * 512 + (ts + 1) * 128],
                            buf[:, icl * 512:(icl + 1) * 512],
                            start=first, stop=(icq == 7 and icl == 7),
                            waits=wl,
                            inc="pe" if (icl == 7 and (icq == 7 or ts == NTS - 1))
                            else None)
                    if icq == 7:
                        last_per_ts[ts] = last_mm
                sd_st.consume(w, last_mm)
                assert last_mm is not None
            for ts in range(NTS):
                dn_evs[(nb, ts)] = dve(
                    lambda e, ts=ts, nb=nb: e.tensor_copy(
                        aof[:, ts * C + nb * 512: ts * C + (nb + 1) * 512],
                        P[ts].ap),
                    waits=[last_per_ts[ts]], reads=[P[ts]])

        rdn_evs = {}
        rd_st.ensure(0, p)
        for nb in range(4):
            rd_st.ensure(nb, p)
            rd_st.ensure(nb + 1, p)
            buf, wtok = rd_st.win(nb)
            last_mm = None
            for ts in range(NTS):
                bank = P[4 + (ts % 2)]
                for ric in range(NRIC):
                    last_mm = mm(
                        bank if ric == 0 else None, bank.ap,
                        hr_s[:, ric * 512 + ts * 128: ric * 512 + (ts + 1) * 128],
                        buf[:, ric * 512:(ric + 1) * 512],
                        start=(ric == 0), stop=(ric == NRIC - 1),
                        waits=maxtok(wtok,
                                     *(hr_ready if (ts == 0 and nb == 0) else []))
                        if ric == 0 else (),
                        inc="pe" if ric == NRIC - 1 else None)
                rdn_evs[(nb, ts)] = dve(
                    lambda e, ts=ts, nb=nb, b=bank: e.scalar_tensor_tensor(
                        aof[:, ts * C + nb * 512: ts * C + (nb + 1) * 512],
                        b.ap, comb_s[:, ts:ts + 1],
                        aof[:, ts * C + nb * 512: ts * C + (nb + 1) * 512],
                        ALU.mult, ALU.add),
                    waits=maxtok(last_mm, dn_evs[(nb, ts)]), reads=[bank])
            rd_st.consume(nb, last_mm)
        f_done = maxtok(*[rdn_evs[(nb, ts)] for nb in range(4) for ts in range(NTS)])

        t_rstd4 = ln_stats(aof, f_done,
                           per_ts=[rdn_evs[(3, ts)] for ts in range(NTS)])
        out_toks = []
        next_xtoks = []
        if p + 1 < NPANEL:
            seq2, lp2 = divmod(p + 1, LPN)
            t02 = lp2 * PANEL
        for ts in range(NTS):
            tz = dve(lambda e, ts=ts: e.tensor_scalar(
                aof[:, ts * C:(ts + 1) * C], aof[:, ts * C:(ts + 1) * C],
                stats[:, 8 + ts:9 + ts], stats[:, 12 + ts:13 + ts],
                ALU.subtract, ALU.mult), waits=[t_rstd4])
            tadd = dve(lambda e, ts=ts: e.tensor_tensor(
                aof[:, ts * C:(ts + 1) * C], aof[:, ts * C:(ts + 1) * C],
                sx[:, ts * C:(ts + 1) * C], ALU.add), waits=[tz])
            out_toks.append(dma(
                "scalar", "ow",
                out[seq, t0 + ts * 128:t0 + (ts + 1) * 128, :],
                aof[:, ts * C:(ts + 1) * C], waits=[tadd]))
            if p + 1 < NPANEL:
                next_xtoks.append(dma(
                    "scalar", "xw", sx[:, ts * C:(ts + 1) * C],
                    x_in[seq2, t02 + ts * 128:t02 + (ts + 1) * 128, :],
                    waits=[tadd]))
        prev_xtoks = next_xtoks
        prev_out_store = out_toks[-1]
        prev_q_done = q_done

    S.emit("sync", lambda e: e.sem_inc(S.sems["sq"], 16),
           waits=[("ow", S.cnt["ow"])])

    with nc.Block() as block:
        @block.sync
        def _(e):
            S.replay("sync", e)

        @block.vector
        def _(e):
            S.replay("vector", e)

        @block.scalar
        def _(e):
            S.replay("scalar", e)

        @block.tensor
        def _(e):
            S.replay("tensor", e)

        @block.gpsimd
        def _(e):
            S.replay("gpsimd", e)

    es.close()
    return nc


_NC_CACHE = None


def kernel(L0_attn_x, L0_attn_W_lat, L0_attn_b_lat, L0_attn_W_d, L0_attn_b_d,
           L0_attn_W_proj, L0_attn_b_proj, L0_ffn_router_weight, L0_ffn_sh_gate,
           L0_ffn_sh_up, L0_ffn_sh_down, L0_ffn_r_gate, L0_ffn_r_up, L0_ffn_r_down):
    global _NC_CACHE
    x = np.ascontiguousarray(np.asarray(L0_attn_x, dtype=np.float32))
    w_lat_a = np.ascontiguousarray(np.asarray(L0_attn_W_lat, dtype=np.float32))
    b_lat_a = np.asarray(L0_attn_b_lat, dtype=np.float32).reshape(1, L)
    w_d_a = np.ascontiguousarray(np.asarray(L0_attn_W_d, dtype=np.float32))
    b_d_a = np.asarray(L0_attn_b_d, dtype=np.float32).reshape(1, C)
    w_proj_a = np.ascontiguousarray(np.asarray(L0_attn_W_proj, dtype=np.float32))
    b_proj_a = np.asarray(L0_attn_b_proj, dtype=np.float32).reshape(1, C)
    rw_a = np.ascontiguousarray(np.asarray(L0_ffn_router_weight, dtype=np.float32).T)
    gw_a = np.ascontiguousarray(np.asarray(L0_ffn_sh_gate, dtype=np.float32).T)
    uw_a = np.ascontiguousarray(np.asarray(L0_ffn_sh_up, dtype=np.float32).T)
    sd_a = np.ascontiguousarray(np.asarray(L0_ffn_sh_down, dtype=np.float32).T)
    rg_a = np.ascontiguousarray(np.asarray(L0_ffn_r_gate, dtype=np.float32).T)
    ru_a = np.ascontiguousarray(np.asarray(L0_ffn_r_up, dtype=np.float32).T)
    rd_a = np.ascontiguousarray(np.asarray(L0_ffn_r_down, dtype=np.float32).T)

    ident = np.eye(128, dtype=ml_dtypes.bfloat16)
    ii, jj = np.meshgrid(np.arange(128), np.arange(128), indexing="ij")
    tri01 = (jj >= ii).astype(ml_dtypes.bfloat16)
    onesc = np.ones((128, 1), dtype=ml_dtypes.bfloat16)
    onesr = np.ones((1, 128), dtype=np.float32)

    if _NC_CACHE is None:
        _NC_CACHE = build_program()
    nc = _NC_CACHE

    in_maps = []
    for c in range(8):
        in_maps.append({
            "x_in": np.ascontiguousarray(x[2 * c:2 * c + 2]),
            "w_lat": w_lat_a, "b_lat": b_lat_a, "w_d": w_d_a, "b_d": b_d_a,
            "w_proj": w_proj_a, "b_proj": b_proj_a, "rw_t": rw_a,
            "gw_t": gw_a, "uw_t": uw_a, "sd_t": sd_a,
            "rg_t": rg_a, "ru_t": ru_a, "rd_t": rd_a,
            "ident_in": ident, "tri_in": tri01, "onesc_in": onesc,
            "onesr_in": onesr,
        })
    res = run_bass_kernel_spmd(nc, in_maps, core_ids=list(range(8)))
    outs = [np.asarray(res.results[c]["out"], dtype=np.float32) for c in range(8)]
    return np.concatenate(outs, axis=0)
